# version 01
# baseline (speedup 1.0000x reference)
import numpy as np
import ml_dtypes
import concourse.bass as bass
import concourse.mybir as mybir
from concourse.bass_utils import run_bass_kernel_spmd

F32 = mybir.dt.float32
BF16 = mybir.dt.bfloat16
AF = mybir.ActivationFunctionType
ALU = mybir.AluOpType
AX = mybir.AxisListType

D = 4096
NH = 32
S = 4096
FF = 11008
CLIP = 8.0
EPS = 1e-5
SCALE = 128 ** -0.5
SLOPES = [2.0 ** (-0.25 * (h + 1)) for h in range(NH)]
EXT = [29, 30, 31, 32, 13, 14, 15, 16]
NEGBIG = -1e9


def _geom(i):
    j = 7 - i
    ownt = [4 * i + t for t in range(4)] + [4 * j + t for t in range(4)]
    fb = sorted(set(range(32)) - set(ownt))
    gq = [4 * j + s for s in range(4)] + [4 * i + s for s in range(4)]
    return j, ownt, fb, gq


def _host_tables(i):
    j, ownt, fb, gq = _geom(i)
    qpos = np.zeros((128, 8), np.float32)
    kb = np.full((8, 36), NEGBIG, np.float32)
    for s in range(8):
        qpos[:, s] = 128 * gq[s] + np.arange(128)
        nf = 24 if s < 4 else 12
        for m in range(nf):
            kb[s, m] = 128.0 * fb[m] if fb[m] < gq[s] else NEGBIG
        for o in range(EXT[s] - nf):
            kb[s, nf + o] = 128.0 * ownt[o]
    return qpos, kb


def _tile_slots(t):
    return 24 if t == 0 else (28 if t == 1 else 4 * (t - 2))


class _Sem:
    __slots__ = ("h", "n", "dma")

    def __init__(self, nc, name, dma):
        self.h = nc.alloc_semaphore(name)
        self.n = 0
        self.dma = dma


class _Em:
    ENGINES = ("tensor", "vector", "scalar", "gpsimd", "sync")

    def __init__(self, nc):
        self.nc = nc
        self.ops = {e: [] for e in self.ENGINES}
        self.known = {e: {} for e in self.ENGINES}
        self.allsems = []

    def on(self, eng, fn, wait=(), inc=None, inck=1):
        kn = self.known[eng]
        rw = []
        for s, v in wait:
            if v <= 0 or kn.get(id(s), -1) >= v:
                continue
            rw.append((s, v))
            kn[id(s)] = v
        if inc is not None:
            inc.n += inck
            if not inc.dma:
                kn[id(inc)] = inc.n
        self.ops[eng].append((fn, tuple(rw), inc, inck))
        return inc.n if inc is not None else None

    def barrier(self):
        snap = [(s, s.n) for s in self.allsems if s.n > 0]
        for e in self.ENGINES:
            for s, v in snap:
                if self.known[e].get(id(s), -1) < v:
                    self.ops[e].append((None, ((s, v),), None, 1))
                    self.known[e][id(s)] = v

    def realize(self, block):
        for e in self.ENGINES:
            ops = self.ops[e]

            def mk(ops=ops):
                def body(eng):
                    for fn, waits, inc, inck in ops:
                        for s, v in waits:
                            eng.wait_ge(s.h, v)
                        if fn is None:
                            continue
                        inst = fn(eng)
                        if inc is not None:
                            inst.then_inc(inc.h, inck)
                return body
            getattr(block, e)(mk())


def build_program(dbg=()):
    nc = bass.Bass("TRN2", target_bir_lowering=False)

    xow = nc.dram_tensor("xow", [D, 1024], F32, kind="ExternalInput").ap()
    xfo = nc.dram_tensor("xfo", [D, 3072], F32, kind="ExternalInput").ap()
    wqkvT = nc.dram_tensor("wqkvT", [D, 3 * D], F32, kind="ExternalInput").ap()
    woutT = nc.dram_tensor("woutT", [D, D], F32, kind="ExternalInput").ap()
    wgT = nc.dram_tensor("wgT", [D, FF], F32, kind="ExternalInput").ap()
    wuT = nc.dram_tensor("wuT", [D, FF], F32, kind="ExternalInput").ap()
    wdT = nc.dram_tensor("wdT", [FF, D], F32, kind="ExternalInput").ap()
    qlnw_in = nc.dram_tensor("qlnw", [128, 32], F32, kind="ExternalInput").ap()
    qlnb_in = nc.dram_tensor("qlnb", [128, 32], F32, kind="ExternalInput").ap()
    klnw_in = nc.dram_tensor("klnw", [128, 32], F32, kind="ExternalInput").ap()
    klnb_in = nc.dram_tensor("klnb", [128, 32], F32, kind="ExternalInput").ap()
    qpos_in = nc.dram_tensor("qpos", [128, 8], F32, kind="ExternalInput").ap()
    kbt_in = nc.dram_tensor("kbt", [8, 128, 36], F32, kind="ExternalInput").ap()
    out = nc.dram_tensor("out", [D, 1024], F32, kind="ExternalOutput").ap()

    def idram(name, shape, dt):
        kind = "ExternalOutput" if name in dbg else "Internal"
        return nc.dram_tensor(name, shape, dt, kind=kind).ap()

    xn_sp = idram("xn_sp", [D, S], BF16)
    qst_sp = idram("qst_sp", [D, 1024], BF16)
    kst_sp = idram("kst_sp", [D, S], BF16)
    qn_sp = idram("qn_sp", [D, 1024], BF16)
    kblk = idram("kblk", [32, NH, 128, 128], BF16)
    vblk = idram("vblk", [32, NH, 128, 128], BF16)
    attn_sp = idram("attn_sp", [D, 1024], BF16)
    aproj_sp = idram("aproj_sp", [D, 1024], F32)
    facc_sp = idram("facc_sp", [D, 1024], F32)

    c_iota = nc.dram_tensor("iota_in", [128, 128], F32, kind="ExternalInput").ap()
    c_tri = nc.dram_tensor("tri_in", [128, 128], F32, kind="ExternalInput").ap()

    base = (nc.sbuf_base + 31) // 32 * 32
    top = nc.sbuf_top
    cur = [base]

    def sal(name, shape, dt, at=None):
        if at is None:
            at = cur[0]
            sz = int(np.prod(shape[1:])) * mybir.dt.size(dt)
            cur[0] = (at + sz + 31) // 32 * 32
        return nc.alloc_sbuf_tensor_at(name, shape, dt, offset=at)

    iota_sb = sal("iota_sb", [128, 128], F32)
    tri_sb = sal("tri_sb", [128, 128], F32)
    ident = sal("ident", [128, 128], BF16)
    ones_f1 = sal("ones_f1", [1, 128], F32)
    ones_bf = sal("ones_bf", [128, 1], BF16)
    qlnw = sal("qlnw_sb", [128, 32], F32)
    qlnb = sal("qlnb_sb", [128, 32], F32)
    klnw = sal("klnw_sb", [128, 32], F32)
    klnb = sal("klnb_sb", [128, 32], F32)
    qlnw_s = sal("qlnw_s", [128, 32], F32)
    qlnb_s = sal("qlnb_s", [128, 32], F32)
    qpos_sb = sal("qpos_sb", [128, 8], F32)
    kbt_sb = sal("kbt_sb", [128, 8, 36], F32)
    in1 = sal("in1", [128, 128], F32)
    in1d = sal("in1d", [128, 128], F32)
    negm = sal("negm", [128, 1], F32)
    sumx = sal("sumx", [128, 1], F32)
    recip = sal("recip", [128, 1], F32)
    t2 = sal("t2", [128, 36], F32)
    R = (cur[0] + 1023) // 1024 * 1024
    K = 1024
    assert top - R > 193 * K, (R, top)

    def at(name, shape, dt, o):
        sz = int(np.prod(shape[1:])) * mybir.dt.size(dt)
        assert R + o + sz <= top - 32, (name, o, sz)
        return sal(name, shape, dt, at=R + o)

    a_xt = [at("a_xt0", [128, 32, 512], F32, 0),
            at("a_xt1", [128, 32, 512], F32, 64 * K)]
    a_xb1 = [at("a_xb10", [128, 512], BF16, 160 * K),
             at("a_xb11", [128, 512], BF16, 161 * K)]
    a_sq1 = [at("a_sq10", [128, 512], BF16, 162 * K),
             at("a_sq11", [128, 512], BF16, 163 * K)]
    a_xf1 = at("a_xf1", [128, 512], F32, 164 * K)
    a_xf2 = at("a_xf2", [128, 512], F32, 160 * K)
    a_xnb = [at("a_xnb0", [128, 512], BF16, 166 * K),
             at("a_xnb1", [128, 512], BF16, 167 * K)]
    a_stage = [at("a_stage0", [128, 512], BF16, 168 * K),
               at("a_stage1", [128, 512], BF16, 169 * K)]
    statrow = at("statrow", [1, 1024], F32, 170 * K)
    e2row = at("e2row", [1, 512], F32, 174 * K)
    svrow = at("svrow", [1, 512], F32, 176 * K)
    mub = at("mub", [128, 512], F32, 178 * K)
    rstdb = at("rstdb", [128, 512], F32, 180 * K)
    varrow = at("varrow", [1, 512], F32, 182 * K)
    a_xng = [at(f"a_xng{t}", [128, 32, 512], BF16, 32 * K * t) for t in range(4)]
    a_cc = [at("a_cc0", [128, 512], BF16, 160 * K),
            at("a_cc1", [128, 512], BF16, 161 * K)]
    a_vt = [at("a_vt0", [128, 512], BF16, 162 * K),
            at("a_vt1", [128, 512], BF16, 163 * K)]
    a_st = [at("a_st0", [128, 32, 512], BF16, 0),
            at("a_st1", [128, 32, 512], BF16, 32 * K)]
    a_wq = [at("a_wq0", [128, 32, 128], BF16, 170 * K),
            at("a_wq1", [128, 32, 128], BF16, 178 * K)]
    c_q = [at("c_q0", [128, 128], BF16, 0), at("c_q1", [128, 128], BF16, 1 * K)]
    c_k = [at("c_k0", [128, 36, 128], BF16, 2 * K),
           at("c_k1", [128, 36, 128], BF16, 12 * K)]
    c_v = [at("c_v0", [128, 36, 128], BF16, 22 * K),
           at("c_v1", [128, 36, 128], BF16, 32 * K)]
    c_s = at("c_s", [128, 36 * 128], F32, 42 * K)
    c_p = at("c_p", [128, 36 * 128], BF16, 61 * K)
    c_pt = at("c_pt", [128, 36 * 128], BF16, 71 * K)
    c_at = at("c_at", [128, 128], BF16, 81 * K)
    c_afm = at("c_afm", [128, 1024], BF16, 82 * K)
    c2_at = at("c2_at", [128, 32, 1024], BF16, 0)
    c2_ao = [at("c2_ao0", [128, 512], F32, 66 * K),
             at("c2_ao1", [128, 512], F32, 68 * K)]
    d_xn = at("d_xn", [128, 32, 1024], BF16, 0)
    d_h = at("d_h", [128, 43, 1024], BF16, 64 * K)
    d_wg = [at("d_wg0", [128, 32, 128], BF16, 150 * K),
            at("d_wg1", [128, 32, 128], BF16, 158 * K)]
    d_wu = [at("d_wu0", [128, 32, 128], BF16, 166 * K),
            at("d_wu1", [128, 32, 128], BF16, 174 * K)]
    d_wd = [at("d_wd0", [128, 43, 128], BF16, 150 * K),
            at("d_wd1", [128, 43, 128], BF16, 161 * K)]
    d_g = [at("d_g0", [128, 512], BF16, 182 * K),
           at("d_g1", [128, 512], BF16, 183 * K)]
    d_fa = at("d_fa", [128, 512], F32, 184 * K)
    d_xc = at("d_xc", [128, 512], F32, 186 * K)
    d_ap = at("d_ap", [128, 512], F32, 188 * K)
    d_o1 = at("d_o1", [128, 512], F32, 190 * K)

    ps = [nc.alloc_psum_tensor(f"ps{k2}", [128, 512], F32) for k2 in range(6)]
    pt = [nc.alloc_psum_tensor(f"pt{k2}", [128, 512], BF16) for k2 in range(2)]

    em = _Em(nc)
    LD = _Sem(nc, "LD", True)
    LW = _Sem(nc, "LW", True)
    SG = _Sem(nc, "SG", True)
    PE = _Sem(nc, "PE", False)
    VE = _Sem(nc, "VE", False)
    SE = _Sem(nc, "SE", False)
    em.allsems = [LD, LW, SG, PE, VE, SE]

    def dma(eng, o, i, sem, waits=()):
        return em.on(eng, lambda e, o=o, i=i: e.dma_start(out=o, in_=i),
                     wait=waits, inc=sem, inck=16)

    xresh = {
        "xn": xn_sp.rearrange("(c p) t -> p c t", p=128),
        "qst": qst_sp.rearrange("(c p) t -> p c t", p=128),
        "kst": kst_sp.rearrange("(c p) t -> p c t", p=128),
        "qn": qn_sp.rearrange("(c p) t -> p c t", p=128),
        "attn": attn_sp.rearrange("(c p) t -> p c t", p=128),
        "aproj": aproj_sp.rearrange("(c p) t -> p c t", p=128),
        "facc": facc_sp.rearrange("(c p) t -> p c t", p=128),
        "xow": xow.rearrange("(c p) t -> p c t", p=128),
        "xfo": xfo.rearrange("(c p) t -> p c t", p=128),
        "out": out.rearrange("(c p) t -> p c t", p=128),
    }

    for dst, src in ((iota_sb, c_iota), (tri_sb, c_tri), (qlnw, qlnw_in),
                     (qlnb, qlnb_in), (klnw, klnw_in), (klnb, klnb_in),
                     (qpos_sb, qpos_in)):
        dma("gpsimd", dst[:, :], src[:, :], LW)
    dma("gpsimd", kbt_sb[:, :, :], kbt_in.rearrange("s p m -> p s m"), LW)
    em.on("gpsimd", lambda e: e.memset(ident[:, :], 1.0))
    em.on("gpsimd", lambda e: e.affine_select(
        ident[:, :], ident[:, :], [[1, 128]], ALU.is_equal, 0.0, base=0,
        channel_multiplier=-1), inc=PE)
    em.on("gpsimd", lambda e: e.memset(ones_f1[:, :], 1.0))
    em.on("gpsimd", lambda e: e.memset(ones_bf[:, :], 1.0), inc=PE)
    init_pe = PE.n
    init_lw = LW.n
    em.on("vector", lambda e: e.tensor_scalar_mul(qlnw_s[:, :], qlnw[:, :], SCALE),
          wait=[(LW, init_lw)])
    em.on("vector", lambda e: e.tensor_scalar_mul(qlnb_s[:, :], qlnb[:, :], SCALE),
          inc=VE)

    def xsrc(t):
        if t < 2:
            return xresh["xow"][:, :, 512 * t:512 * (t + 1)]
        return xresh["xfo"][:, :, 512 * (t - 2):512 * (t - 1)]

    state = {"ve_mu": 0, "ve_bc": 0}

    def ln_finalize(pe_stats):
        em.on("vector", lambda e: e.tensor_scalar_mul(
            statrow[0:1, 0:512], ps[0][0:1, :], 1.0 / D), wait=[(PE, pe_stats)])
        em.on("vector", lambda e: e.tensor_scalar_mul(
            e2row[0:1, :], ps[1][0:1, :], 1.0 / D), inc=VE)
        state["ve_mu"] = VE.n
        em.on("scalar", lambda e: e.square(svrow[0:1, :], statrow[0:1, 0:512]),
              wait=[(VE, state["ve_mu"])], inc=SE)
        em.on("vector", lambda e: e.scalar_tensor_tensor(
            varrow[0:1, :], e2row[0:1, :], EPS, svrow[0:1, :],
            ALU.add, ALU.subtract), wait=[(SE, SE.n)], inc=VE)
        em.on("scalar", lambda e: e.activation(svrow[0:1, :], varrow[0:1, :],
                                               AF.Sqrt, bias=0.0, scale=1.0),
              wait=[(VE, VE.n)], inc=SE)
        em.on("vector", lambda e: e.reciprocal(statrow[0:1, 512:1024],
                                               svrow[0:1, :]),
              wait=[(SE, SE.n)], inc=VE)
        bw = [(VE, VE.n), (VE, state["ve_bc"]), (PE, init_pe)]
        em.on("tensor", lambda e: e.matmul(ps[2][:, :], ones_f1[:, :],
                                           statrow[0:1, 0:512],
                                           start=True, stop=True), wait=bw)
        em.on("tensor", lambda e: e.matmul(ps[3][:, :], ones_f1[:, :],
                                           statrow[0:1, 512:1024],
                                           start=True, stop=True), inc=PE)
        pe_bc = PE.n
        em.on("vector", lambda e: e.tensor_copy(mub[:, :], ps[2][:, :]),
              wait=[(PE, pe_bc)])
        em.on("vector", lambda e: e.tensor_copy(rstdb[:, :], ps[3][:, :]), inc=VE)
        state["ve_bc"] = VE.n

    ve_apply = [0] * 8
    se_tile = [0] * 8
    pe_stats_done = [0] * 8
    sg_xn = [0] * 8
    for t in range(8):
        par = t % 2
        w = []
        if t >= 2:
            w += [(VE, ve_apply[t - 2]), (SE, se_tile[t - 2]),
                  (PE, pe_stats_done[t - 2])]
        ld_x = dma("sync", a_xt[par][:, :, :], xsrc(t), LD, w)
        pe_c = []
        for c in range(32):
            cw = [(LD, ld_x)]
            if c >= 2:
                cw.append((PE, pe_c[c - 2]))
            em.on("vector", lambda e, c=c, par=par: e.tensor_copy(
                a_xb1[c % 2][:, :], a_xt[par][:, c, :]), wait=cw, inc=VE)
            vcnt = VE.n
            em.on("scalar", lambda e, c=c, par=par: e.square(
                a_sq1[c % 2][:, :], a_xt[par][:, c, :]), wait=cw, inc=SE)
            scnt = SE.n
            mw = [(VE, vcnt), (SE, scnt), (PE, init_pe)]
            if c == 0:
                mw.append((VE, state["ve_mu"]))
            em.on("tensor", lambda e, c=c: e.matmul(
                ps[0][0:1, :], ones_bf[:, :], a_xb1[c % 2][:, :],
                start=(c == 0), stop=(c == 31)), wait=mw)
            em.on("tensor", lambda e, c=c: e.matmul(
                ps[1][0:1, :], ones_bf[:, :], a_sq1[c % 2][:, :],
                start=(c == 0), stop=(c == 31)), inc=PE)
            pe_c.append(PE.n)
        pe_stats_done[t] = PE.n
        ln_finalize(pe_stats_done[t])
        aw = [(VE, state["ve_bc"])]
        xnb_free = {}
        for c in range(32):
            cpar = c % 2
            cw = list(aw if c == 0 else ()) + [xnb_free.get(cpar, (SG, 0))]
            em.on("vector", lambda e, c=c, par=par: e.tensor_sub(
                a_xf1[:, :], a_xt[par][:, c, :], mub[:, :]), wait=cw)
            em.on("vector", lambda e, c=c, cpar=cpar: e.tensor_mul(
                a_xnb[cpar][:, :], a_xf1[:, :], rstdb[:, :]), inc=VE)
            sgn = dma("gpsimd", xresh["xn"][:, c, 512 * t:512 * (t + 1)],
                      a_xnb[cpar][:, :], SG, [(VE, VE.n)])
            xnb_free[cpar] = (SG, sgn)
        ve_apply[t] = VE.n
        se_tile[t] = SE.n
        sg_xn[t] = SG.n
    em.barrier()

    pe_wq = {}
    accfree = {}
    vtfree = {}
    ptfree = {}
    psfree = {}
    epar = [0]
    pe_group = [0, 0]
    for G in range(2):
        tiles = [0, 1, 2, 3] if G == 0 else [4, 5, 6, 7]
        ld_xng = {}
        for tt, t in enumerate(tiles):
            w = [(SG, sg_xn[t])]
            if G == 1:
                w.append((PE, pe_group[0]))
            ld_xng[t] = dma("sync", a_xng[tt][:, :, :],
                            xresh["xn"][:, :, 512 * t:512 * (t + 1)], LD, w)
        for e2 in (range(0, 96) if G == 0 else range(32, 96)):
            par = epar[0] % 2
            epar[0] += 1
            ldw = dma("gpsimd", a_wq[par][:, :, :],
                      wqkvT[:, 128 * e2:128 * (e2 + 1)]
                      .rearrange("(c p) eo -> p c eo", p=128), LW,
                      [(PE, pe_wq.get(par, 0))])
            use = tiles if e2 >= 32 else [0, 1]
            for t in use:
                ti = tiles.index(t)
                pp = (e2 + t) % 2
                mw = [(LW, ldw), (LD, ld_xng[t]), psfree.get(pp, (PE, 0))]
                for c in range(32):
                    em.on("tensor", lambda e, c=c, par=par, ti=ti, pp=pp: e.matmul(
                        ps[pp][:, :], a_wq[par][:, c, :], a_xng[ti][:, c, :],
                        start=(c == 0), stop=(c == 31)),
                        wait=mw if c == 0 else (), inc=PE if c == 31 else None)
                pe_mm = PE.n
                pe_wq[par] = pe_mm
                cw = [(PE, pe_mm), accfree.get(pp, (VE, 0))]
                em.on("vector", lambda e, pp=pp: e.tensor_scalar(
                    a_cc[pp][:, :], ps[pp][:, :], CLIP, -CLIP, ALU.min, ALU.max),
                    wait=cw, inc=VE)
                psfree[pp] = (VE, VE.n)
                ve_cc = VE.n
                if e2 < 64:
                    key = "qst" if e2 < 32 else "kst"
                    ch = e2 if e2 < 32 else e2 - 32
                    sgn = dma("gpsimd", xresh[key][:, ch, 512 * t:512 * (t + 1)],
                              a_cc[pp][:, :], SG, [(VE, ve_cc)])
                    accfree[pp] = (SG, sgn)
                else:
                    h = e2 - 64
                    tpar = pp
                    tw = [(VE, ve_cc), ptfree.get(tpar, (VE, 0)), (PE, init_pe)]
                    for q4 in range(4):
                        em.on("tensor", lambda e, q4=q4, pp=pp, tpar=tpar: e.matmul(
                            pt[tpar][:, 128 * q4:128 * (q4 + 1)],
                            a_cc[pp][:, 128 * q4:128 * (q4 + 1)], ident[:, :],
                            is_transpose=True, start=(q4 == 0), stop=(q4 == 3)),
                            wait=tw if q4 == 0 else (),
                            inc=PE if q4 == 3 else None)
                    pe_t = PE.n
                    accfree[pp] = (PE, pe_t)
                    vw = [(PE, pe_t), vtfree.get(tpar, (SG, 0))]
                    em.on("vector", lambda e, tpar=tpar: e.tensor_copy(
                        a_vt[tpar][:, :], pt[tpar][:, :]), wait=vw, inc=VE)
                    ptfree[tpar] = (VE, VE.n)
                    s0 = _tile_slots(t)
                    sgn = dma("gpsimd",
                              vblk[s0:s0 + 4, h, :, :].rearrange("m p c -> p m c"),
                              a_vt[tpar].ap().rearrange("p (m c) -> p m c", m=4),
                              SG, [(VE, VE.n)])
                    vtfree[tpar] = (SG, sgn)
        pe_group[G] = PE.n
    em.barrier()

    jobs = [("k", t) for t in range(8)] + [("q", 0), ("q", 1)]
    stfree = {}
    stagefree = {}
    for ji, (kind, t) in enumerate(jobs):
        par = ji % 2
        src = xresh["kst"] if kind == "k" else xresh["qst"]
        cols = (512 * t, 512 * (t + 1))
        ld_st = dma("sync", a_st[par][:, :, :], src[:, :, cols[0]:cols[1]], LD,
                    [stfree.get(par, (VE, 0))])
        pe_c = []
        for c in range(32):
            cw = [(LD, ld_st)]
            if c >= 2:
                cw.append((PE, pe_c[c - 2]))
            em.on("scalar", lambda e, c=c, par=par: e.square(
                a_sq1[c % 2][:, :], a_st[par][:, c, :]), wait=cw, inc=SE)
            scnt = SE.n
            mw = [(SE, scnt), (LD, ld_st)]
            if c == 0:
                mw.append((VE, state["ve_mu"]))
            em.on("tensor", lambda e, c=c, par=par: e.matmul(
                ps[0][0:1, :], ones_bf[:, :], a_st[par][:, c, :],
                start=(c == 0), stop=(c == 31)), wait=mw)
            em.on("tensor", lambda e, c=c: e.matmul(
                ps[1][0:1, :], ones_bf[:, :], a_sq1[c % 2][:, :],
                start=(c == 0), stop=(c == 31)), inc=PE)
            pe_c.append(PE.n)
        ln_finalize(PE.n)
        wv = (qlnw_s, qlnb_s) if kind == "q" else (klnw, klnb)
        for c in range(32):
            spar = c % 2
            aw = [(VE, state["ve_bc"]), stagefree.get(spar, (SG, 0))]
            em.on("vector", lambda e, c=c, par=par: e.tensor_sub(
                a_xf1[:, :], a_st[par][:, c, :], mub[:, :]), wait=aw)
            em.on("vector", lambda e: e.tensor_mul(
                a_xf2[:, :], a_xf1[:, :], rstdb[:, :]))
            em.on("vector", lambda e, c=c, spar=spar, wv=wv: e.tensor_scalar(
                a_stage[spar][:, :], a_xf2[:, :], wv[0][:, c:c + 1],
                wv[1][:, c:c + 1], ALU.mult, ALU.add), inc=VE)
            vn = VE.n
            if kind == "q":
                sgn = dma("gpsimd", xresh["qn"][:, c, cols[0]:cols[1]],
                          a_stage[spar][:, :], SG, [(VE, vn)])
            else:
                s0 = _tile_slots(t)
                sgn = dma("gpsimd",
                          kblk[s0:s0 + 4, c, :, :].rearrange("m p c -> p m c"),
                          a_stage[spar].ap().rearrange("p (m c) -> p m c", m=4),
                          SG, [(VE, vn)])
            stagefree[spar] = (SG, sgn)
        stfree[par] = (VE, VE.n)
    em.barrier()

    qcol = [512 + 128 * s3 for s3 in range(4)] + [128 * s4 for s4 in range(4)]
    cbuf_free = {}
    ptg_free = {}
    se_exp_prev = 0
    se_at = 0
    sg_afm = 0
    for h in range(NH):
        hw = [(LW, init_lw), (VE, VE.n)]
        em.on("vector", lambda e, h=h: e.tensor_scalar_mul(
            in1[:, :], iota_sb[:, :], SLOPES[h]), wait=hw)
        em.on("vector", lambda e: e.tensor_add(in1d[:, :], in1[:, :],
                                               tri_sb[:, :]), inc=VE)
        ve_in1 = VE.n
        for s2 in range(8):
            E = EXT[s2]
            par = (h * 8 + s2) % 2
            lw = [cbuf_free.get(par, (PE, 0))]
            ld_q = dma("sync", c_q[par][:, :],
                       xresh["qn"][:, h, qcol[s2]:qcol[s2] + 128], LD, lw)
            if s2 < 4:
                ld_k = dma("sync", c_k[par][:, 0:E, :],
                           kblk[0:E, h, :, :].rearrange("m p c -> p m c"), LD, lw)
                ld_v = dma("sync", c_v[par][:, 0:E, :],
                           vblk[0:E, h, :, :].rearrange("m p c -> p m c"), LD, lw)
            else:
                dma("sync", c_k[par][:, 0:12, :],
                    kblk[0:12, h, :, :].rearrange("m p c -> p m c"), LD, lw)
                ld_k = dma("sync", c_k[par][:, 12:E, :],
                           kblk[24:12 + E, h, :, :].rearrange("m p c -> p m c"),
                           LD, lw)
                dma("sync", c_v[par][:, 0:12, :],
                    vblk[0:12, h, :, :].rearrange("m p c -> p m c"), LD, lw)
                ld_v = dma("sync", c_v[par][:, 12:E, :],
                           vblk[24:12 + E, h, :, :].rearrange("m p c -> p m c"),
                           LD, lw)
            em.on("vector", lambda e, s2=s2, h=h, E=E: e.tensor_scalar(
                t2[:, 0:E], kbt_sb[:, s2, 0:E], qpos_sb[:, s2:s2 + 1],
                SLOPES[h], ALU.subtract, ALU.mult), wait=[(LW, init_lw)])
            nch = (E + 3) // 4
            for cc in range(nch):
                n = min(512, (E - 4 * cc) * 128)
                pp = cc % 2
                mw = [(LD, ld_q), (LD, ld_k), psfree.get(pp, (VE, 0)),
                      (SE, se_exp_prev)]
                em.on("tensor", lambda e, cc=cc, n=n, pp=pp, par=par: e.matmul(
                    ps[pp][:, 0:n], c_q[par][:, :],
                    c_k[par][:, 4 * cc:4 * cc + n // 128, :],
                    start=True, stop=True), wait=mw, inc=PE)
                pe_qk = PE.n
                for u in range(n // 128):
                    m = 4 * cc + u
                    which = in1d if m == E - 1 else in1
                    em.on("vector",
                          lambda e, m=m, u=u, pp=pp, which=which:
                          e.scalar_tensor_tensor(
                              c_s[:, 128 * m:128 * (m + 1)],
                              ps[pp][:, 128 * u:128 * (u + 1)],
                              t2[:, m:m + 1], which[:, :], ALU.add, ALU.add),
                          wait=[(PE, pe_qk), (VE, ve_in1), (SE, se_exp_prev)],
                          inc=VE if u == n // 128 - 1 else None)
                psfree[pp] = (VE, VE.n)
            em.on("vector", lambda e, E=E: e.tensor_reduce(
                negm[:, :], c_s[:, 0:128 * E], AX.X, ALU.max, negate=True),
                wait=[(SE, se_exp_prev)], inc=VE)
            ve_max = VE.n
            ew = [(VE, ve_max), ptg_free.get("exp", (PE, 0))]
            em.on("scalar", lambda e, E=E: e.activation(
                c_p[:, 0:128 * E], c_s[:, 0:128 * E], AF.Exp, bias=negm[:, 0:1],
                scale=1.0, accum_out=sumx[:, :]), wait=ew, inc=SE)
            se_exp_prev = SE.n
            em.on("vector", lambda e: e.reciprocal(recip[:, :], sumx[:, :]),
                  wait=[(SE, se_exp_prev), (SE, se_at)], inc=VE)
            ve_recip = VE.n
            ngrp = (E + 3) // 4
            last_pt_pe = 0
            for g in range(ngrp):
                gpar = g % 2
                cnt = min(4, E - 4 * g)
                tw = [(SE, se_exp_prev), ptg_free.get(gpar, (VE, 0)),
                      (PE, init_pe)]
                for u in range(cnt):
                    m = 4 * g + u
                    em.on("tensor", lambda e, m=m, u=u, gpar=gpar, cnt=cnt:
                          e.matmul(pt[gpar][:, 128 * u:128 * (u + 1)],
                                   c_p[:, 128 * m:128 * (m + 1)], ident[:, :],
                                   is_transpose=True, start=(u == 0),
                                   stop=(u == cnt - 1)),
                          wait=tw if u == 0 else (),
                          inc=PE if u == cnt - 1 else None)
                pe_ptg = PE.n
                last_pt_pe = pe_ptg
                em.on("vector", lambda e, g=g, gpar=gpar, cnt=cnt: e.tensor_copy(
                    c_pt[:, 512 * g:512 * g + 128 * cnt],
                    pt[gpar][:, 0:128 * cnt]), wait=[(PE, pe_ptg)], inc=VE)
                ptg_free[gpar] = (VE, VE.n)
            ptg_free["exp"] = (PE, last_pt_pe)
            ve_pt_all = VE.n
            pvw = [(VE, ve_pt_all), (LD, ld_v), (SE, se_at)]
            for m in range(E):
                em.on("tensor", lambda e, m=m, par=par: e.matmul(
                    ps[5][:, 0:128], c_pt[:, 128 * m:128 * (m + 1)],
                    c_v[par][:, m, :], start=(m == 0), stop=(m == E - 1)),
                    wait=pvw if m == 0 else (), inc=PE if m == E - 1 else None)
            pe_pv = PE.n
            cbuf_free[par] = (PE, pe_pv)
            em.on("scalar", lambda e: e.activation(
                c_at[:, :], ps[5][:, 0:128], AF.Copy, bias=0.0,
                scale=recip[:, 0:1]), wait=[(PE, pe_pv), (VE, ve_recip)], inc=SE)
            se_at = SE.n
            em.on("tensor", lambda e: e.matmul(
                pt[0][:, 0:128], c_at[:, :], ident[:, :], is_transpose=True,
                start=True, stop=True),
                wait=[(SE, se_at), ptg_free.get(0, (VE, 0))], inc=PE)
            pe_att = PE.n
            aw = [(PE, pe_att)]
            if s2 == 0:
                aw.append((SG, sg_afm))
            em.on("vector", lambda e, s2=s2: e.tensor_copy(
                c_afm[:, qcol[s2]:qcol[s2] + 128], pt[0][:, 0:128]),
                wait=aw, inc=VE)
            ptg_free[0] = (VE, VE.n)
        sg_afm = dma("gpsimd", xresh["attn"][:, h, :], c_afm[:, :], SG,
                     [(VE, VE.n)])
    em.barrier()

    ld_at = dma("sync", c2_at[:, :, :], xresh["attn"][:, :, :], LD,
                [(SG, SG.n)])
    pe_wo = {}
    aofree = {}
    for do in range(32):
        par = do % 2
        ldw = dma("gpsimd", a_wq[par][:, :, :],
                  woutT[:, 128 * do:128 * (do + 1)]
                  .rearrange("(c p) eo -> p c eo", p=128), LW,
                  [(PE, pe_wo.get(par, 0))])
        for tt in range(2):
            pp = (2 * do + tt) % 2
            mw = [(LW, ldw), (LD, ld_at), psfree.get(pp, (VE, 0))]
            for c in range(32):
                em.on("tensor", lambda e, c=c, par=par, tt=tt, pp=pp: e.matmul(
                    ps[pp][:, :], a_wq[par][:, c, :],
                    c2_at[:, c, 512 * tt:512 * (tt + 1)],
                    start=(c == 0), stop=(c == 31)),
                    wait=mw if c == 0 else (), inc=PE if c == 31 else None)
            pe_wo[par] = PE.n
            em.on("vector", lambda e, pp=pp: e.tensor_copy(
                c2_ao[pp][:, :], ps[pp][:, :]),
                wait=[(PE, pe_wo[par]), aofree.get(pp, (SG, 0))], inc=VE)
            psfree[pp] = (VE, VE.n)
            sgn = dma("gpsimd", xresh["aproj"][:, do, 512 * tt:512 * (tt + 1)],
                      c2_ao[pp][:, :], SG, [(VE, VE.n)])
            aofree[pp] = (SG, sgn)
    em.barrier()

    ld_xn2 = dma("sync", d_xn[:, :, :], xresh["xn"][:, :, 0:1024], LD,
                 [(SG, SG.n)])
    wfree = {}
    bankfree = {}
    gfree = {}
    pe_wd_all = 0
    sg_fa = {}
    sg_fa_all = 0
    sg_o = 0
    ve_last_dfa = 0
    for fg in range(2):
        fcs = list(range(0, 43)) if fg == 0 else list(range(43, 86))
        for fi, fc in enumerate(fcs):
            par = fi % 2
            gw = [wfree.get(("g", par), (PE, 0))]
            if fg == 1 and fi < 2:
                gw.append((PE, pe_wd_all))
            ldg = dma("gpsimd", d_wg[par][:, :, :],
                      wgT[:, 128 * fc:128 * (fc + 1)]
                      .rearrange("(c p) eo -> p c eo", p=128), LW, gw)
            ldu = dma("gpsimd", d_wu[par][:, :, :],
                      wuT[:, 128 * fc:128 * (fc + 1)]
                      .rearrange("(c p) eo -> p c eo", p=128), LW, gw)
            for tt in range(2):
                psg = 2 * tt
                psu = psg + 1
                mw = [(LW, ldu), (LD, ld_xn2), bankfree.get(psg, (SE, 0)),
                      bankfree.get(psu, (VE, 0))]
                for c in range(32):
                    em.on("tensor", lambda e, c=c, par=par, tt=tt, psg=psg:
                          e.matmul(ps[psg][:, :], d_wg[par][:, c, :],
                                   d_xn[:, c, 512 * tt:512 * (tt + 1)],
                                   start=(c == 0), stop=(c == 31)),
                          wait=mw if c == 0 else ())
                    em.on("tensor", lambda e, c=c, par=par, tt=tt, psu=psu:
                          e.matmul(ps[psu][:, :], d_wu[par][:, c, :],
                                   d_xn[:, c, 512 * tt:512 * (tt + 1)],
                                   start=(c == 0), stop=(c == 31)),
                          inc=PE if c == 31 else None)
                pe_gu = PE.n
                wfree[("g", par)] = (PE, pe_gu)
                em.on("scalar", lambda e, tt=tt, psg=psg: e.activation(
                    d_g[tt][:, :], ps[psg][:, :], AF.Silu, bias=0.0, scale=1.0),
                    wait=[(PE, pe_gu), gfree.get(tt, (VE, 0))], inc=SE)
                se_s = SE.n
                bankfree[psg] = (SE, se_s)
                em.on("vector", lambda e, fi=fi, tt=tt, psu=psu: e.tensor_mul(
                    d_h[:, fi, 512 * tt:512 * (tt + 1)], ps[psu][:, :],
                    d_g[tt][:, :]), wait=[(SE, se_s)], inc=VE)
                bankfree[psu] = (VE, VE.n)
                gfree[tt] = (VE, VE.n)
        ve_h = VE.n
        pe_gu_all = PE.n
        for dc in range(32):
            par = dc % 2
            ldd = dma("gpsimd", d_wd[par][:, :, :],
                      wdT[43 * 128 * fg:43 * 128 * (fg + 1),
                          128 * dc:128 * (dc + 1)]
                      .rearrange("(f p) dd -> p f dd", p=128), LW,
                      [wfree.get(("d", par), (PE, 0)), (PE, pe_gu_all)])
            for tt in range(2):
                pp = 4 + tt
                mw = [(LW, ldd), (VE, ve_h), bankfree.get(pp, (VE, 0))]
                for f in range(43):
                    em.on("tensor", lambda e, f=f, par=par, tt=tt, pp=pp:
                          e.matmul(ps[pp][:, :], d_wd[par][:, f, :],
                                   d_h[:, f, 512 * tt:512 * (tt + 1)],
                                   start=(f == 0), stop=(f == 42)),
                          wait=mw if f == 0 else (),
                          inc=PE if f == 42 else None)
                pe_dn = PE.n
                wfree[("d", par)] = (PE, pe_dn)
                cols = (512 * tt, 512 * (tt + 1))
                if fg == 0:
                    em.on("vector", lambda e, pp=pp: e.tensor_copy(
                        d_fa[:, :], ps[pp][:, :]),
                        wait=[(PE, pe_dn), sg_fa.get("d_fa", (SG, 0))], inc=VE)
                    vn = VE.n
                    bankfree[pp] = (VE, vn)
                    sgn = dma("gpsimd", xresh["facc"][:, dc, cols[0]:cols[1]],
                              d_fa[:, :], SG, [(VE, vn)])
                    sg_fa["d_fa"] = (SG, sgn)
                    sg_fa_all = sgn
                else:
                    lf = dma("sync", d_fa[:, :],
                             xresh["facc"][:, dc, cols[0]:cols[1]], LD,
                             [(SG, sg_fa_all), (VE, ve_last_dfa)])
                    lx = dma("sync", d_xc[:, :],
                             xresh["xow"][:, dc, cols[0]:cols[1]], LD,
                             [(VE, ve_last_dfa)])
                    la = dma("sync", d_ap[:, :],
                             xresh["aproj"][:, dc, cols[0]:cols[1]], LD,
                             [(VE, ve_last_dfa)])
                    em.on("vector", lambda e, pp=pp: e.tensor_add(
                        d_o1[:, :], ps[pp][:, :], d_fa[:, :]),
                        wait=[(PE, pe_dn), (LD, lf), (SG, sg_o)])
                    em.on("vector", lambda e: e.tensor_add(
                        d_o1[:, :], d_o1[:, :], d_xc[:, :]), wait=[(LD, lx)])
                    em.on("vector", lambda e: e.tensor_add(
                        d_o1[:, :], d_o1[:, :], d_ap[:, :]), wait=[(LD, la)],
                        inc=VE)
                    vn = VE.n
                    ve_last_dfa = vn
                    bankfree[pp] = (VE, vn)
                    sg_o = dma("gpsimd", xresh["out"][:, dc, cols[0]:cols[1]],
                               d_o1[:, :], SG, [(VE, vn)])
        pe_wd_all = PE.n
    final_sg = SG.n
    em.on("gpsimd", lambda e: e.nop(), wait=[(SG, final_sg)])

    with nc.Block("main") as block:
        em.realize(block)
    return nc


_NC_CACHE = {}


def _get_program(dbg=()):
    key = tuple(sorted(dbg))
    if key not in _NC_CACHE:
        _NC_CACHE[key] = build_program(dbg)
    return _NC_CACHE[key]


def make_in_maps(inputs):
    x = np.asarray(inputs["L0_attn_hidden_states"], np.float32)
    wqkvT = np.ascontiguousarray(np.asarray(inputs["L0_attn_Wqkv"], np.float32).T)
    woutT = np.ascontiguousarray(np.asarray(inputs["L0_attn_Wout"], np.float32).T)
    wgT = np.ascontiguousarray(np.asarray(inputs["L0_ffn_W_gate"], np.float32).T)
    wuT = np.ascontiguousarray(np.asarray(inputs["L0_ffn_W_up"], np.float32).T)
    wdT = np.ascontiguousarray(np.asarray(inputs["L0_ffn_W_down"], np.float32).T)

    def lnvec(v):
        return np.ascontiguousarray(np.asarray(v, np.float32).reshape(32, 128).T)

    qlnw = lnvec(inputs["L0_attn_q_ln_w"])
    qlnb = lnvec(inputs["L0_attn_q_ln_b"])
    klnw = lnvec(inputs["L0_attn_k_ln_w"])
    klnb = lnvec(inputs["L0_attn_k_ln_b"])

    in_maps = []
    for c in range(8):
        b, i = c // 4, c % 4
        j, ownt, fb, gq = _geom(i)
        xT = x[b].T
        own_cols = np.r_[512 * i:512 * (i + 1), 512 * j:512 * (j + 1)]
        fo_cols = np.concatenate([np.arange(128 * g, 128 * (g + 1)) for g in fb])
        qpos, kb = _host_tables(i)
        kbt = np.ascontiguousarray(
            np.broadcast_to(kb[:, None, :], (8, 128, 36)), dtype=np.float32)
        in_maps.append({
            "xow": np.ascontiguousarray(xT[:, own_cols]),
            "xfo": np.ascontiguousarray(xT[:, fo_cols]),
            "wqkvT": wqkvT, "woutT": woutT, "wgT": wgT, "wuT": wuT, "wdT": wdT,
            "qlnw": qlnw, "qlnb": qlnb, "klnw": klnw, "klnb": klnb,
            "qpos": qpos, "kbt": kbt,
            "iota_in": np.tile(np.arange(128, dtype=np.float32), (128, 1)),
            "tri_in": np.where(np.arange(128)[None, :] > np.arange(128)[:, None],
                               -1e30, 0.0).astype(np.float32),
        })
    return in_maps


def kernel(**inputs):
    nc = _get_program()
    in_maps = make_in_maps(inputs)
    res = run_bass_kernel_spmd(nc, in_maps, core_ids=list(range(8)))
    x = np.asarray(inputs["L0_attn_hidden_states"], np.float32)
    out = np.empty((2, S, D), np.float32)
    for c in range(8):
        b, i = c // 4, c % 4
        j = 7 - i
        r = res.results[c]["out"]
        out[b, 512 * i:512 * (i + 1), :] = r[:, 0:512].T
        out[b, 512 * j:512 * (j + 1), :] = r[:, 512:1024].T
    return out
